# version 01
# baseline (speedup 1.0000x reference)
import numpy as np
import concourse.bass as bass
import concourse.mybir as mybir
from concourse.bass_utils import run_bass_kernel_spmd

F32 = mybir.dt.float32
ALU = mybir.AluOpType
AF = mybir.ActivationFunctionType
AX = mybir.AxisListType.X

T = 2048
C = 1024
NT = T // 128
NC = C // 128
W = 512
NWIN = 4
STRIDE = 256
E = 8
HID = 2048
NHC = HID // 128


class Sem:
    def __init__(self, h):
        self.h = h
        self.n = 0


class Plan:
    def __init__(self):
        self.q = {k: [] for k in ("sync", "tensor", "vector", "scalar", "gpsimd")}

    def emit(self, eng, fn):
        self.q[eng].append(fn)

    def wait(self, eng, sem, val=None):
        v = sem.n if val is None else val
        assert v <= sem.n, f"wait for future value {v} > {sem.n}"
        if v > 0:
            self.emit(eng, lambda e, s=sem.h, v=v: e.wait_ge(s, v))

    def dma(self, eng, out, in_, sem):
        sem.n += 16
        self.emit(eng, lambda e, o=out, i=in_, s=sem.h: e.dma_start(out=o, in_=i).then_inc(s, 16))

    def op(self, eng, fn, sem=None):
        if sem is not None:
            sem.n += 1
            self.emit(eng, lambda e, f=fn, s=sem.h: f(e).then_inc(s, 1))
        else:
            self.emit(eng, lambda e, f=fn: f(e))


def build_program(debug=False):
    nc = bass.Bass("TRN2", target_bir_lowering=False)

    x_h = nc.dram_tensor("x", [T, C], F32, kind="ExternalInput").ap()
    wqkv_h = nc.dram_tensor("wqkv", [C, 3 * C], F32, kind="ExternalInput").ap()
    bqkv_h = nc.dram_tensor("bqkv", [128, 24], F32, kind="ExternalInput").ap()
    bvrow_h = nc.dram_tensor("bvrow", [8, C], F32, kind="ExternalInput").ap()
    wout_h = nc.dram_tensor("wout", [C, C], F32, kind="ExternalInput").ap()
    bout_h = nc.dram_tensor("bout", [128, NC], F32, kind="ExternalInput").ap()
    w12t_h = nc.dram_tensor("w12t", [E, C, 2 * HID], F32, kind="ExternalInput").ap()
    w3t_h = nc.dram_tensor("w3t", [E, HID, C], F32, kind="ExternalInput").ap()
    gwt_h = nc.dram_tensor("gwt", [C, E], F32, kind="ExternalInput").ap()
    eye_h = nc.dram_tensor("eye", [128, 128], F32, kind="ExternalInput").ap()
    ramp_h = nc.dram_tensor("ramp", [128, E], F32, kind="ExternalInput").ap()
    sel_h = nc.dram_tensor("sel", [8, 8 * 128], F32, kind="ExternalInput").ap()
    out_h = nc.dram_tensor("out", [T, C], F32, kind="ExternalOutput").ap()
    dbg = {}
    if debug:
        dbg["xn_fm"] = nc.dram_tensor("dbg_xn_fm", [128, NC * T], F32, kind="ExternalOutput").ap()
        dbg["p_fm"] = nc.dram_tensor("dbg_p_fm", [8, T], F32, kind="ExternalOutput").ap()
        dbg["y_attn"] = nc.dram_tensor("dbg_y_attn", [128, NC * T], F32, kind="ExternalOutput").ap()
        dbg["qkv"] = nc.dram_tensor("dbg_qkv", [128, 3 * W], F32, kind="ExternalOutput").ap()
        dbg["prob"] = nc.dram_tensor("dbg_prob", [128, W], F32, kind="ExternalOutput").ap()
        dbg["pt"] = nc.dram_tensor("dbg_pt", [128, 4 * W], F32, kind="ExternalOutput").ap()
        dbg["opre"] = nc.dram_tensor("dbg_opre", [64, 1024], F32, kind="ExternalOutput").ap()
        dbg["qkv3"] = nc.dram_tensor("dbg_qkv3", [128, 3 * W], F32, kind="ExternalOutput").ap()
        dbg["stat"] = nc.dram_tensor("dbg_stat", [128, 176], F32, kind="ExternalOutput").ap()

    p = Plan()

    from contextlib import ExitStack
    with ExitStack() as _stack:
        _ec = _stack.enter_context
        xn_fm = _ec(nc.sbuf_tensor("xn_fm", [128, NC * T], F32))
        y_fm = _ec(nc.sbuf_tensor("y_fm", [128, NC * T], F32))
        scr = _ec(nc.sbuf_tensor("scr", [128, 16896], F32))
        bqkv_sb = _ec(nc.sbuf_tensor("bqkv_sb", [128, 24], F32))
        gwt_sb = _ec(nc.sbuf_tensor("gwt_sb", [128, NC * E], F32))
        p_fm = _ec(nc.sbuf_tensor("p_fm", [8, T], F32))
        stat = _ec(nc.sbuf_tensor("stat", [128, 176], F32))
        eye_sb = _ec(nc.sbuf_tensor("eye_sb", [128, 128], F32))
        ramp_sb = _ec(nc.sbuf_tensor("ramp_sb", [128, E], F32))
        sel_sb = _ec(nc.sbuf_tensor("sel_sb", [8, 8 * 128], F32))
        bout_sb = _ec(nc.sbuf_tensor("bout_sb", [128, NC], F32))
        xt0 = scr[:, 12800:13824]
        xt1 = scr[:, 13824:14848]
        ob0 = scr[:, 14848:15872]
        ob1 = scr[:, 15872:16896]
        sqjunk = scr[:, 14848:15872]
        xnt0 = scr[:, 14848:15872]
        xnt1 = scr[:, 15872:16896]
        scg_sb = scr[:, 0:NT * E]
        pw_sb = scr[:, 256:256 + NT * E]
        z1b = scr[:, 1024:1152]
        one1b = scr[:, 1152:1280]
        sc2b = scr[:, 1280:1408]
        z2b = scr[:, 1408:1536]
        one2b = scr[:, 1536:1664]
        wq_sl0 = scr[:, 0:1024]
        wq_sl1 = scr[:, 1024:2048]
        wk_sl0 = scr[:, 2048:3072]
        wk_sl1 = scr[:, 3072:4096]
        wv_sl0 = scr[:, 4096:5120]
        wv_sl1 = scr[:, 5120:6144]
        wo_sl0 = scr[0:64, 6144:8192]
        q_sb = scr[:, 8192:8704]
        k_sb = scr[:, 8704:9216]
        v_sb = scr[:, 9216:9728]
        opre = scr[0:64, 9728:10752]
        prob = scr[:, 10752:11264]
        pt_sb = scr[:, 11264:13312]
        bv_sb = scr[0:8, 14848:15872]
        hid_sb = scr[:, 0:NHC * W]
        w12_sl0 = scr[:, 8192:10240]
        w12_sl1 = scr[:, 10240:12288]
        w3_sl0 = scr[:, 12288:14336]
        w3_sl1 = scr[:, 14336:16384]
        pb_sb = scr[:, 16384:16896]
        psA = _ec(nc.psum_tensor("psA", [128, 512], F32))
        psB = _ec(nc.psum_tensor("psB", [128, 512], F32))
        psC = _ec(nc.psum_tensor("psC", [128, 512], F32))
        psD = _ec(nc.psum_tensor("psD", [128, 512], F32))
        psE = _ec(nc.psum_tensor("psE", [128, 512], F32))
        psT = _ec(nc.psum_tensor("psT", [128, 1024], F32))
        psG = _ec(nc.psum_tensor("psG", [128, 512], F32))
        _s_in = _ec(nc.semaphore("s_in"))
        _s_w0 = _ec(nc.semaphore("s_w0"))
        _s_w1 = _ec(nc.semaphore("s_w1"))
        _s_t = _ec(nc.semaphore("s_t"))
        _s_v = _ec(nc.semaphore("s_v"))
        _s_s = _ec(nc.semaphore("s_s"))
        _s_st = _ec(nc.semaphore("s_st"))
        block = _ec(nc.Block())
        s_in = Sem(_s_in)
        s_w0 = Sem(_s_w0)
        s_w1 = Sem(_s_w1)
        s_t = Sem(_s_t)
        s_v = Sem(_s_v)
        s_s = Sem(_s_s)
        s_st = Sem(_s_st)
        wsem = [s_w0, s_w1]

        def vgap():
            p.op("vector", lambda e: e.tensor_copy(stat[:, 174:175], stat[:, 175:176]))

        def sgap():
            p.op("scalar", lambda e: e.copy(stat[:, 173:174], stat[:, 175:176]))

        def vgap2():
            p.op("vector", lambda e: e.tensor_copy(scr[:, 15104:15360], scr[:, 14848:15104]))
            p.op("vector", lambda e: e.tensor_copy(scr[:, 15104:15360], scr[:, 14848:15104]))

        S_SUM, S_SS, S_MEAN, S_VAR, S_VPE, S_SD, S_R0, S_U, S_RSTD, S_NMR = (
            0, 16, 32, 48, 64, 80, 96, 112, 128, 144)

        p.op("vector", lambda e: e.memset(stat[:, 160:176], 0.0))
        p.dma("sync", eye_sb[:], eye_h, s_in)
        p.dma("sync", ramp_sb[:], ramp_h, s_in)
        p.dma("sync", sel_sb[:], sel_h, s_in)
        p.dma("sync", bqkv_sb[:], bqkv_h, s_in)
        p.dma("sync", bout_sb[:], bout_h, s_in)
        p.dma("sync", gwt_sb[:].rearrange("p (dc e) -> p dc e", dc=NC),
              gwt_h.rearrange("(dc p) e -> p dc e", p=128), s_in)

        xb = [xt0, xt1]
        xb_free = [None, None]
        for ti in range(NT):
            b = ti % 2
            if xb_free[b] is not None:
                p.wait("sync", *xb_free[b])
            p.dma("sync", xb[b][:], x_h[ti * 128:(ti + 1) * 128, :], s_in)
            ld_at = s_in.n
            p.wait("vector", s_in, ld_at)
            p.op("vector", lambda e, xb=xb[b], ti=ti: e.reduce_sum(
                stat[:, S_SUM + ti:S_SUM + ti + 1], xb[:], axis=AX))
            p.op("vector", lambda e, xb=xb[b]: e.tensor_mul(sqjunk[:], xb[:], xb[:]))
            p.op("vector", lambda e, ti=ti: e.reduce_sum(
                stat[:, S_SS + ti:S_SS + ti + 1], sqjunk[:], axis=AX), s_v)
            xb_free[b] = (s_v, s_v.n)
        vgap2()
        vgap2()
        ST = lambda g: stat[:, g:g + 16]
        p.op("vector", lambda e: e.tensor_scalar_mul(ST(S_MEAN), ST(S_SUM), 1.0 / C))
        vgap2()
        p.op("vector", lambda e: e.tensor_mul(ST(S_U), ST(S_MEAN), ST(S_MEAN)))
        vgap2()
        p.op("vector", lambda e: e.scalar_tensor_tensor(ST(S_VAR), ST(S_SS), 1.0 / C, ST(S_U),
                                                        ALU.mult, ALU.subtract))
        vgap2()
        p.op("vector", lambda e: e.tensor_scalar_add(ST(S_VPE), ST(S_VAR), 1e-5), s_v)
        p.wait("scalar", s_v, s_v.n)
        sgap()
        sgap()
        p.op("scalar", lambda e: e.activation(ST(S_SD), ST(S_VPE), AF.Sqrt), s_s)
        p.wait("vector", s_s, s_s.n)
        vgap2()
        p.op("vector", lambda e: e.reciprocal(ST(S_R0), ST(S_SD)))
        vgap2()
        p.op("vector", lambda e: e.tensor_mul(ST(S_U), ST(S_R0), ST(S_R0)))
        vgap2()
        p.op("vector", lambda e: e.tensor_mul(ST(S_U), ST(S_U), ST(S_VPE)))
        vgap2()
        p.op("vector", lambda e: e.tensor_scalar(ST(S_U), ST(S_U), -0.5, 1.5, ALU.mult, ALU.add))
        vgap2()
        p.op("vector", lambda e: e.tensor_mul(ST(S_RSTD), ST(S_R0), ST(S_U)))
        vgap2()
        p.op("vector", lambda e: e.tensor_mul(ST(S_NMR), ST(S_MEAN), ST(S_RSTD)))
        vgap2()
        p.op("vector", lambda e: e.tensor_scalar_mul(ST(S_NMR), ST(S_NMR), -1.0))
        vgap2()
        p.op("vector", lambda e: e.tensor_copy(stat[:, 160:161], stat[:, 161:162]), s_v)
        stats_at = s_v.n
        if debug:
            p.wait("sync", s_v, stats_at)
            p.dma("sync", dbg["stat"], stat[:], s_st)
            p.wait("vector", s_st, s_st.n)
            p.wait("tensor", s_st, s_st.n)
            p.wait("sync", s_st, s_st.n)

        xnb = [xnt0, xnt1]
        xnb_free = [None, None]
        xb_free = [None, None]
        for ti in range(NT):
            b = ti % 2
            if xb_free[b] is not None:
                p.wait("sync", *xb_free[b])
            p.dma("sync", xb[b][:], x_h[ti * 128:(ti + 1) * 128, :], s_in)
            ld_at = s_in.n
            p.wait("vector", s_in, ld_at)
            if ti == 0:
                p.wait("vector", s_v, stats_at)
            if xnb_free[b] is not None:
                p.wait("vector", *xnb_free[b])
            p.op("vector", lambda e, xb=xb[b], xn=xnb[b], ti=ti: e.tensor_scalar(
                xn[:], xb[:], stat[:, S_RSTD + ti:S_RSTD + ti + 1],
                stat[:, S_NMR + ti:S_NMR + ti + 1], ALU.mult, ALU.add), s_v)
            xb_free[b] = (s_v, s_v.n)
            xnt_at = s_v.n
            p.wait("tensor", s_v, xnt_at)
            for cc in range(NC):
                p.op("tensor", lambda e, cc=cc, xn=xnb[b]: e.transpose(
                    psT[:, cc * 128:(cc + 1) * 128], xn[:, cc * 128:(cc + 1) * 128], eye_sb[:]),
                    s_t if cc == NC - 1 else None)
            tp_at = s_t.n
            xnb_free[b] = (s_t, tp_at)
            p.wait("vector", s_t, tp_at)
            p.op("vector", lambda e, ti=ti: e.tensor_copy(
                xn_fm[:].rearrange("p (cc t) -> p cc t", cc=NC)[:, :, ti * 128:(ti + 1) * 128],
                psT[:].rearrange("p (cc t) -> p cc t", cc=NC)), s_v)
            evac_at = s_v.n
            p.wait("tensor", s_v, evac_at)
            for dc in range(NC):
                p.op("tensor", lambda e, dc=dc, ti=ti: e.matmul(
                    psG[:, ti * 8: ti * 8 + 8],
                    xn_fm[:, dc * T + ti * 128: dc * T + ti * 128 + 128],
                    gwt_sb[:, dc * E:(dc + 1) * E],
                    start=(ti == 0 and dc == 0), stop=(ti == NT - 1 and dc == NC - 1)),
                    s_t if (dc == NC - 1) else None)
        gate_done = s_t.n

        if debug:
            p.wait("sync", s_v, s_v.n)
            p.dma("sync", dbg["xn_fm"], xn_fm[:], s_st)
            p.wait("vector", s_st, s_st.n)
            p.wait("tensor", s_st, s_st.n)
            p.wait("sync", s_st, s_st.n)

        M1, R1, M2, R2, D12, P1, P2 = 0, 16, 32, 48, 64, 80, 96
        p.wait("vector", s_t, gate_done)
        p.op("vector", lambda e: e.tensor_copy(scg_sb[:], psG[:, 0:NT * E]), s_v)
        vgap()
        for ti in range(NT):
            p.op("vector", lambda e, ti=ti: e.reduce_max(
                stat[:, M1 + ti:M1 + ti + 1], scg_sb[:, ti * E:(ti + 1) * E], axis=AX))
        for ti in range(NT):
            p.op("vector", lambda e, ti=ti: e.tensor_scalar(
                z1b[:, ti * E:(ti + 1) * E], scg_sb[:, ti * E:(ti + 1) * E],
                stat[:, M1 + ti:M1 + ti + 1], None, ALU.is_equal))
        for ti in range(NT):
            p.op("vector", lambda e, ti=ti: e.tensor_mul(
                z1b[:, ti * E:(ti + 1) * E], z1b[:, ti * E:(ti + 1) * E], ramp_sb[:]))
        for ti in range(NT):
            p.op("vector", lambda e, ti=ti: e.reduce_max(
                stat[:, R1 + ti:R1 + ti + 1], z1b[:, ti * E:(ti + 1) * E], axis=AX))
        for ti in range(NT):
            p.op("vector", lambda e, ti=ti: e.tensor_scalar(
                one1b[:, ti * E:(ti + 1) * E], ramp_sb[:],
                stat[:, R1 + ti:R1 + ti + 1], None, ALU.is_equal))
        for ti in range(NT):
            p.op("vector", lambda e, ti=ti: e.scalar_tensor_tensor(
                sc2b[:, ti * E:(ti + 1) * E], one1b[:, ti * E:(ti + 1) * E], -1e30,
                scg_sb[:, ti * E:(ti + 1) * E], ALU.mult, ALU.add))
        for ti in range(NT):
            p.op("vector", lambda e, ti=ti: e.reduce_max(
                stat[:, M2 + ti:M2 + ti + 1], sc2b[:, ti * E:(ti + 1) * E], axis=AX))
        for ti in range(NT):
            p.op("vector", lambda e, ti=ti: e.tensor_scalar(
                z2b[:, ti * E:(ti + 1) * E], sc2b[:, ti * E:(ti + 1) * E],
                stat[:, M2 + ti:M2 + ti + 1], None, ALU.is_equal))
        for ti in range(NT):
            p.op("vector", lambda e, ti=ti: e.tensor_mul(
                z2b[:, ti * E:(ti + 1) * E], z2b[:, ti * E:(ti + 1) * E], ramp_sb[:]))
        for ti in range(NT):
            p.op("vector", lambda e, ti=ti: e.reduce_max(
                stat[:, R2 + ti:R2 + ti + 1], z2b[:, ti * E:(ti + 1) * E], axis=AX))
        for ti in range(NT):
            p.op("vector", lambda e, ti=ti: e.tensor_scalar(
                one2b[:, ti * E:(ti + 1) * E], ramp_sb[:],
                stat[:, R2 + ti:R2 + ti + 1], None, ALU.is_equal))
        p.op("vector", lambda e: e.tensor_sub(stat[:, D12:D12 + 16], stat[:, M1:M1 + 16],
                                              stat[:, M2:M2 + 16]), s_v)
        p.wait("scalar", s_v, s_v.n)
        sgap()
        sgap()
        p.op("scalar", lambda e: e.activation(stat[:, P1:P1 + 16], stat[:, D12:D12 + 16],
                                              AF.Sigmoid), s_s)
        p.wait("vector", s_s, s_s.n)
        vgap()
        p.op("vector", lambda e: e.tensor_scalar(stat[:, P2:P2 + 16], stat[:, P1:P1 + 16],
                                                 -1.0, 1.0, ALU.mult, ALU.add))
        vgap()
        for ti in range(NT):
            p.op("vector", lambda e, ti=ti: e.tensor_scalar_mul(
                pw_sb[:, ti * E:(ti + 1) * E], one1b[:, ti * E:(ti + 1) * E],
                stat[:, P1 + ti:P1 + ti + 1]))
        for ti in range(NT):
            p.op("vector", lambda e, ti=ti: e.scalar_tensor_tensor(
                pw_sb[:, ti * E:(ti + 1) * E], one2b[:, ti * E:(ti + 1) * E],
                stat[:, P2 + ti:P2 + ti + 1], pw_sb[:, ti * E:(ti + 1) * E],
                ALU.mult, ALU.add), s_v if ti == NT - 1 else None)
        pw_done = s_v.n
        p.wait("tensor", s_v, pw_done)
        for rnd in range(2):
            for j in range(8):
                ti = rnd * 8 + j
                p.op("tensor", lambda e, ti=ti, j=j: e.transpose(
                    psT[0:8, j * 128:(j + 1) * 128], pw_sb[:, ti * E:(ti + 1) * E], eye_sb[:]),
                    s_t if j == 7 else None)
            p.wait("vector", s_t, s_t.n)
            p.op("vector", lambda e, rnd=rnd: e.tensor_copy(
                p_fm[:, rnd * 1024:(rnd + 1) * 1024], psT[0:8, :]), s_v)
            if rnd == 0:
                p.wait("tensor", s_v, s_v.n)
        routing_done = s_v.n
        if debug:
            p.wait("sync", s_v, routing_done)
            p.dma("sync", dbg["p_fm"], p_fm[:], s_st)
            p.wait("vector", s_st, s_st.n)
            p.wait("tensor", s_st, s_st.n)
            p.wait("sync", s_st, s_st.n)

        def load_wslice(dst, col0, ncols, sem):
            p.dma("sync", dst[:, 0:NC * ncols].rearrange("p (dc j) -> p dc j", dc=NC),
                  wqkv_h.rearrange("(dc p) m -> p dc m", p=128)[:, :, col0:col0 + ncols], sem)

        wq_b = [wq_sl0, wq_sl1]
        wk_b = [wk_sl0, wk_sl1]
        wv_b = [wv_sl0, wv_sl1]
        wfree = [(s_v, routing_done), (s_v, routing_done)]
        wofree = [None]
        p.wait("sync", s_v, routing_done)
        p.dma("sync", bv_sb[:], bvrow_h, s_in)
        bv_at = s_in.n

        for win in range(NWIN):
            t0 = win * STRIDE
            for hp in range(8):
                b = hp % 2
                if wfree[b] is not None:
                    p.wait("sync", *wfree[b])
                load_wslice(wq_b[b], hp * 128, 128, wsem[b])
                load_wslice(wk_b[b], C + hp * 128, 128, wsem[b])
                load_wslice(wv_b[b], 2 * C + hp * 128, 128, wsem[b])
                wld_at = wsem[b].n

                p.wait("tensor", wsem[b], wld_at)
                if win == 0 and hp == 0:
                    p.wait("tensor", s_in, bv_at)
                p.wait("tensor", s_v, s_v.n)
                for dc in range(NC):
                    p.op("tensor", lambda e, dc=dc, b=b, t0=t0: e.matmul(
                        psA[:], wq_b[b][:, dc * 128:(dc + 1) * 128],
                        xn_fm[:, dc * T + t0: dc * T + t0 + W],
                        start=(dc == 0), stop=(dc == NC - 1)), s_t if dc == NC - 1 else None)
                qmm_at = s_t.n
                for dc in range(NC):
                    p.op("tensor", lambda e, dc=dc, b=b, t0=t0: e.matmul(
                        psB[:], wk_b[b][:, dc * 128:(dc + 1) * 128],
                        xn_fm[:, dc * T + t0: dc * T + t0 + W],
                        start=(dc == 0), stop=(dc == NC - 1)), s_t if dc == NC - 1 else None)
                kmm_at = s_t.n
                for ts in range(4):
                    for dc in range(NC):
                        p.op("tensor", lambda e, ts=ts, dc=dc, b=b, t0=t0: e.matmul(
                            psE[:, ts * 128:(ts + 1) * 128],
                            xn_fm[:, dc * T + t0 + ts * 128: dc * T + t0 + ts * 128 + 128],
                            wv_b[b][:, dc * 128:(dc + 1) * 128],
                            start=(ts == 0 and dc == 0), stop=False))
                    p.op("tensor", lambda e, ts=ts, hp=hp: e.matmul(
                        psE[:, ts * 128:(ts + 1) * 128], sel_sb[:, 0:128],
                        bv_sb[:, hp * 128:(hp + 1) * 128],
                        start=False, stop=(ts == 3)), s_t if ts == 3 else None)
                vmm_at = s_t.n

                p.wait("vector", s_t, qmm_at)
                p.op("vector", lambda e, hp=hp: e.tensor_scalar_add(
                    q_sb[:], psA[:], bqkv_sb[:, hp:hp + 1]), s_v)
                p.wait("vector", s_t, kmm_at)
                p.op("vector", lambda e, hp=hp: e.tensor_scalar_add(
                    k_sb[:], psB[:], bqkv_sb[:, 8 + hp:9 + hp]), s_v)
                p.wait("vector", s_t, vmm_at)
                p.op("vector", lambda e: e.tensor_copy(v_sb[:], psE[:]), s_v)
                wfree[b] = (s_v, s_v.n)
                qkv_at = s_v.n
                if debug and win == 3 and hp == 0:
                    p.wait("sync", s_v, qkv_at)
                    p.dma("sync", dbg["qkv3"][:, 0:W], q_sb[:], s_st)
                    p.dma("sync", dbg["qkv3"][:, W:2 * W], k_sb[:], s_st)
                    p.dma("sync", dbg["qkv3"][:, 2 * W:3 * W], v_sb[:], s_st)
                    p.wait("vector", s_st, s_st.n)
                    p.wait("tensor", s_st, s_st.n)
                    p.wait("sync", s_st, s_st.n)
                if debug and win == 0 and hp == 0:
                    p.wait("sync", s_v, qkv_at)
                    p.dma("sync", dbg["qkv"][:, 0:W], q_sb[:], s_st)
                    p.wait("vector", s_st, s_st.n)
                    p.wait("tensor", s_st, s_st.n)
                    p.wait("sync", s_st, s_st.n)
                    p.dma("sync", dbg["qkv"][:, W:2 * W], k_sb[:], s_st)
                    p.dma("sync", dbg["qkv"][:, 2 * W:3 * W], v_sb[:], s_st)
                    p.wait("vector", s_st, s_st.n)
                    p.wait("tensor", s_st, s_st.n)
                    p.wait("sync", s_st, s_st.n)

                for hh in range(2):
                    pofs = 64 * hh
                    for qs in range(4):
                        ps = psC if qs % 2 == 0 else psD
                        p.wait("tensor", s_v, qkv_at)
                        if qs >= 2:
                            p.wait("tensor", s_v, s_v.n)
                        p.op("tensor", lambda e, ps=ps, pofs=pofs, qs=qs: e.matmul(
                            ps[:], q_sb[pofs:pofs + 64, qs * 128:(qs + 1) * 128],
                            k_sb[pofs:pofs + 64, :], start=True, stop=True), s_t)
                        sc_at = s_t.n
                        p.wait("vector", s_t, sc_at)
                        p.op("vector", lambda e, ps=ps: e.reduce_max(
                            stat[:, 0:1], ps[:], axis=AX, negate=True))
                        vgap()
                        p.op("vector", lambda e: e.tensor_scalar_mul(
                            stat[:, 1:2], stat[:, 0:1], 0.125), s_v)
                        p.wait("scalar", s_v, s_v.n)
                        sgap()
                        p.op("scalar", lambda e, ps=ps: e.activation(
                            prob[:], ps[:], AF.Exp, bias=stat[:, 1:2], scale=0.125), s_s)
                        p.wait("vector", s_s, s_s.n)
                        p.op("vector", lambda e: e.reduce_sum(stat[:, 2:3], prob[:], axis=AX))
                        vgap()
                        p.op("vector", lambda e: e.reciprocal(stat[:, 3:4], stat[:, 2:3]))
                        vgap()
                        p.op("vector", lambda e: e.tensor_scalar_mul(
                            prob[:], prob[:], stat[:, 3:4]), s_v)
                        prob_at = s_v.n
                        p.wait("tensor", s_v, prob_at)
                        for ks in range(4):
                            p.op("tensor", lambda e, ks=ks, qs=qs: e.transpose(
                                psT[:, (qs % 2) * 512 + ks * 128:(qs % 2) * 512 + (ks + 1) * 128],
                                prob[:, ks * 128:(ks + 1) * 128], eye_sb[:]),
                                s_t if ks == 3 else None)
                        tp_at = s_t.n
                        p.wait("vector", s_t, tp_at)
                        p.op("vector", lambda e, qs=qs: e.tensor_copy(
                            pt_sb[:].rearrange("p (ks q) -> p ks q", ks=4)[:, :, qs * 128:(qs + 1) * 128],
                            psT[:, (qs % 2) * 512:(qs % 2) * 512 + 512].rearrange(
                                "p (ks q) -> p ks q", ks=4)), s_v)
                        if debug and win == 0 and hp == 0 and hh == 0 and qs == 0:
                            p.wait("sync", s_v, s_v.n)
                            p.dma("sync", dbg["prob"], prob[:], s_st)
                            p.wait("vector", s_st, s_st.n)
                            p.wait("tensor", s_st, s_st.n)
                            p.wait("sync", s_st, s_st.n)
                    pt_done = s_v.n
                    if debug and win == 0 and hp == 0 and hh == 0:
                        p.wait("sync", s_v, pt_done)
                        p.dma("sync", dbg["pt"], pt_sb[:], s_st)
                        p.wait("vector", s_st, s_st.n)
                        p.wait("tensor", s_st, s_st.n)
                        p.wait("sync", s_st, s_st.n)
                    p.wait("tensor", s_v, pt_done)
                    for ks in range(4):
                        p.op("tensor", lambda e, ks=ks, hh=hh: e.matmul(
                            psG[0:64, :], v_sb[:, ks * 128 + 64 * hh: ks * 128 + 64 * hh + 64],
                            pt_sb[:, ks * W:(ks + 1) * W],
                            start=(ks == 0), stop=(ks == 3)), s_t if ks == 3 else None)
                    av_at = s_t.n
                    p.wait("vector", s_t, av_at)
                    p.op("vector", lambda e, hh=hh: e.tensor_copy(
                        opre[:, hh * 512:(hh + 1) * 512], psG[0:64, :]), s_v)
                ow_at = s_v.n
                if debug and win == 0 and hp == 0:
                    p.wait("sync", s_v, ow_at)
                    p.dma("sync", dbg["opre"], opre[:], s_st)
                    p.wait("vector", s_st, s_st.n)
                    p.wait("tensor", s_st, s_st.n)
                    p.wait("sync", s_st, s_st.n)

                if wofree[0] is not None:
                    p.wait("sync", *wofree[0])
                p.dma("sync", wo_sl0[:].rearrange("p (hh m) -> p hh m", hh=2),
                      wout_h.rearrange("(g p) m -> g p m", p=64)[2 * hp:2 * hp + 2].rearrange(
                          "g p m -> p g m"), wsem[0])
                wo_at = wsem[0].n
                p.wait("tensor", wsem[0], wo_at)
                p.wait("tensor", s_v, ow_at)
                for cc in range(NC):
                    ps = psA if cc % 2 == 0 else psB
                    p.wait("tensor", s_v, s_v.n)
                    for hh in range(2):
                        p.op("tensor", lambda e, ps=ps, cc=cc, hh=hh: e.matmul(
                            ps[:], wo_sl0[:, hh * 1024 + cc * 128: hh * 1024 + (cc + 1) * 128],
                            opre[:, hh * 512:(hh + 1) * 512],
                            start=(hh == 0), stop=(hh == 1)), s_t if hh == 1 else None)
                    op_at = s_t.n
                    p.wait("vector", s_t, op_at)
                    if hp == 0:
                        p.op("vector", lambda e, ps=ps, cc=cc, win=win: e.tensor_scalar_add(
                            y_fm[:, cc * T + win * W: cc * T + (win + 1) * W], ps[:],
                            bout_sb[:, cc:cc + 1]), s_v)
                    else:
                        p.op("vector", lambda e, ps=ps, cc=cc, win=win: e.tensor_add(
                            y_fm[:, cc * T + win * W: cc * T + (win + 1) * W],
                            y_fm[:, cc * T + win * W: cc * T + (win + 1) * W], ps[:]), s_v)
                wofree[0] = (s_v, s_v.n)
        attn_done = s_v.n
        if debug:
            p.wait("sync", s_v, attn_done)
            p.dma("sync", dbg["y_attn"], y_fm[:], s_st)
            p.wait("vector", s_st, s_st.n)
            p.wait("tensor", s_st, s_st.n)
            p.wait("sync", s_st, s_st.n)

        w12_b = [w12_sl0, w12_sl1]
        w3_b = [w3_sl0, w3_sl1]
        w12free = [(s_v, attn_done), (s_v, attn_done)]
        w3free = [(s_v, attn_done), (s_v, attn_done)]
        w12i = 0
        w3i = 0
        for tc in range(4):
            for e_ix in range(E):
                p.wait("tensor", s_v, s_v.n)
                p.op("tensor", lambda e, e_ix=e_ix, tc=tc: e.matmul(
                    psG[:], sel_sb[:, e_ix * 128:(e_ix + 1) * 128],
                    p_fm[0:8, tc * W:(tc + 1) * W],
                    start=True, stop=True), s_t)
                p.wait("vector", s_t, s_t.n)
                p.op("vector", lambda e: e.tensor_copy(pb_sb[:], psG[:]), s_v)
                pb_at = s_v.n

                for hc in range(NHC):
                    b = w12i % 2
                    w12i += 1
                    if w12free[b] is not None:
                        p.wait("sync", *w12free[b])
                    p.dma("sync", w12_b[b][:, 0:C].rearrange("p (dc j) -> p dc j", dc=NC),
                          w12t_h[e_ix].rearrange("(dc p) m -> p dc m", p=128)[
                              :, :, hc * 128:(hc + 1) * 128], wsem[b])
                    p.dma("sync", w12_b[b][:, C:2 * C].rearrange("p (dc j) -> p dc j", dc=NC),
                          w12t_h[e_ix].rearrange("(dc p) m -> p dc m", p=128)[
                              :, :, HID + hc * 128: HID + (hc + 1) * 128], wsem[b])
                    w12_at = wsem[b].n
                    ps2 = psA if hc % 2 == 0 else psB
                    ps1 = psB if hc % 2 == 0 else psA
                    p.wait("tensor", wsem[b], w12_at)
                    p.wait("tensor", s_v, s_v.n)
                    for dc in range(NC):
                        p.op("tensor", lambda e, dc=dc, b=b, ps2=ps2, tc=tc: e.matmul(
                            ps2[:], w12_b[b][:, C + dc * 128: C + (dc + 1) * 128],
                            xn_fm[:, dc * T + tc * W: dc * T + (tc + 1) * W],
                            start=(dc == 0), stop=(dc == NC - 1)), s_t if dc == NC - 1 else None)
                    h2_at = s_t.n
                    for dc in range(NC):
                        p.op("tensor", lambda e, dc=dc, b=b, ps1=ps1, tc=tc: e.matmul(
                            ps1[:], w12_b[b][:, dc * 128:(dc + 1) * 128],
                            xn_fm[:, dc * T + tc * W: dc * T + (tc + 1) * W],
                            start=(dc == 0), stop=(dc == NC - 1)), s_t if dc == NC - 1 else None)
                    h1_at = s_t.n
                    p.wait("vector", s_t, h2_at)
                    p.wait("vector", s_v, pb_at)
                    p.op("vector", lambda e, ps2=ps2, hc=hc: e.tensor_mul(
                        hid_sb[:, hc * W:(hc + 1) * W], ps2[:], pb_sb[:]), s_v)
                    p.wait("scalar", s_t, h1_at)
                    p.op("scalar", lambda e, ps1=ps1: e.activation(
                        psT[:, 0:512], ps1[:], AF.Silu), s_s)
                    p.wait("vector", s_s, s_s.n)
                    p.op("vector", lambda e, hc=hc: e.tensor_mul(
                        hid_sb[:, hc * W:(hc + 1) * W], hid_sb[:, hc * W:(hc + 1) * W],
                        psT[:, 0:512]), s_v)
                    w12free[b] = (s_v, s_v.n)
                hid_at = s_v.n

                for dd in range(NC):
                    b = w3i % 2
                    w3i += 1
                    if w3free[b] is not None:
                        p.wait("sync", *w3free[b])
                    p.dma("sync", w3_b[b][:].rearrange("p (hcb j) -> p hcb j", hcb=NHC),
                          w3t_h[e_ix].rearrange("(hcb p) m -> p hcb m", p=128)[
                              :, :, dd * 128:(dd + 1) * 128], wsem[b])
                    w3_at = wsem[b].n
                    ps = psC if dd % 2 == 0 else psD
                    p.wait("tensor", wsem[b], w3_at)
                    p.wait("tensor", s_v, s_v.n)
                    for hc in range(NHC):
                        p.op("tensor", lambda e, hc=hc, b=b, ps=ps: e.matmul(
                            ps[:], w3_b[b][:, hc * 128:(hc + 1) * 128],
                            hid_sb[:, hc * W:(hc + 1) * W],
                            start=(hc == 0), stop=(hc == NHC - 1)), s_t if hc == NHC - 1 else None)
                    w3mm_at = s_t.n
                    p.wait("vector", s_t, w3mm_at)
                    p.op("vector", lambda e, ps=ps, dd=dd, tc=tc: e.tensor_add(
                        y_fm[:, dd * T + tc * W: dd * T + (tc + 1) * W],
                        y_fm[:, dd * T + tc * W: dd * T + (tc + 1) * W], ps[:]), s_v)
                    w3free[b] = (s_v, s_v.n)
        ffn_done = s_v.n

        xb_free = [(s_v, ffn_done), (s_v, ffn_done)]
        obs = [ob0, ob1]
        ob_store_at = [None, None]
        for ti in range(NT):
            b = ti % 2
            if xb_free[b] is not None:
                p.wait("sync", *xb_free[b])
            p.dma("sync", xb[b][:], x_h[ti * 128:(ti + 1) * 128, :], s_in)
            xld_at = s_in.n
            p.wait("tensor", s_v, s_v.n)
            for cc in range(NC):
                p.op("tensor", lambda e, cc=cc, ti=ti: e.transpose(
                    psT[:, cc * 128:(cc + 1) * 128],
                    y_fm[:, cc * T + ti * 128: cc * T + ti * 128 + 128], eye_sb[:]),
                    s_t if cc == NC - 1 else None)
            ytp_at = s_t.n
            p.wait("vector", s_t, ytp_at)
            p.wait("vector", s_in, xld_at)
            if ob_store_at[b] is not None:
                p.wait("vector", s_st, ob_store_at[b])
            p.op("vector", lambda e, ob=obs[b], b=b: e.tensor_add(ob[:], xb[b][:], psT[:]), s_v)
            xb_free[b] = (s_v, s_v.n)
            add_at = s_v.n
            p.wait("sync", s_v, add_at)
            p.dma("sync", out_h[ti * 128:(ti + 1) * 128, :], obs[b][:], s_st)
            ob_store_at[b] = s_st.n
        p.wait("sync", s_st, s_st.n)

        @block.sync
        def _(eng):
            for f in p.q["sync"]:
                f(eng)

        @block.tensor
        def _(eng):
            for f in p.q["tensor"]:
                f(eng)

        @block.vector
        def _(eng):
            for f in p.q["vector"]:
                f(eng)

        @block.scalar
        def _(eng):
            for f in p.q["scalar"]:
                f(eng)

        @block.gpsimd
        def _(eng):
            for f in p.q["gpsimd"]:
                f(eng)

    return nc


_RAMP = np.tile(np.arange(8, 0, -1, dtype=np.float32), (128, 1))
_EYE = np.eye(128, dtype=np.float32)
_SEL = np.zeros((8, 8 * 128), dtype=np.float32)
for _e in range(8):
    _SEL[_e, _e * 128:(_e + 1) * 128] = 1.0


def kernel(L0_attn_x, L0_attn_w_qkv, L0_attn_b_qkv, L0_attn_w_out, L0_attn_b_out,
           L0_ffn_w12, L0_ffn_w3, L0_ffn_gate_w):
    x = np.ascontiguousarray(np.asarray(L0_attn_x, dtype=np.float32))
    wqkv = np.ascontiguousarray(np.asarray(L0_attn_w_qkv, dtype=np.float32))
    bqkv = np.ascontiguousarray(np.asarray(L0_attn_b_qkv, dtype=np.float32))
    wout = np.ascontiguousarray(np.asarray(L0_attn_w_out, dtype=np.float32))
    bout = np.ascontiguousarray(np.asarray(L0_attn_b_out, dtype=np.float32))
    w12t = np.ascontiguousarray(np.asarray(L0_ffn_w12, dtype=np.float32).transpose(0, 2, 1))
    w3t = np.ascontiguousarray(np.asarray(L0_ffn_w3, dtype=np.float32).transpose(0, 2, 1))
    gwt = np.ascontiguousarray(np.asarray(L0_ffn_gate_w, dtype=np.float32).T)
    bqkv_r = np.ascontiguousarray(bqkv.reshape(24, 128).T)
    bvrow = np.zeros((8, 1024), dtype=np.float32)
    bvrow[0] = bqkv[2048:3072]
    bout_r = np.ascontiguousarray(bout.reshape(8, 128).T)

    nc = build_program(debug=False)
    shared = {"wqkv": wqkv, "bqkv": bqkv_r, "bvrow": bvrow, "wout": wout, "bout": bout_r,
              "w12t": w12t, "w3t": w3t, "gwt": gwt,
              "eye": _EYE, "ramp": _RAMP, "sel": _SEL}
    in_maps = [dict(shared, x=np.ascontiguousarray(x[i])) for i in range(8)]
    res = run_bass_kernel_spmd(nc, in_maps, core_ids=list(range(8)))
    out = np.stack([res.results[i]["out"] for i in range(8)], axis=0)
    return out.astype(np.float32)
